# version 03
# baseline (speedup 1.0000x reference)
import numpy as np
from contextlib import ExitStack

import concourse.bass as bass
import concourse.mybir as mybir
from concourse.bass_utils import run_bass_kernel_spmd

F32 = mybir.dt.float32
BF16 = mybir.dt.bfloat16
AF = mybir.ActivationFunctionType
ALU = mybir.AluOpType
AX = mybir.AxisListType

P = 128
L = 2048
D = 1024
NH = 16
HD = 64
FF = 4096
NE = 8
NC = D // P
NTC = L // P
TT = 512
NTT = L // TT
NHC = FF // P
CHUNK_TILES = 32
CHUNK_COLS = CHUNK_TILES * P
IP_TILES = 16
IP_COLS = IP_TILES * P
SC = 0.125

ENGINES = ("tensor", "vector", "scalar", "gpsimd", "sync")
TAPS = None
_CACHE = {}


def _tile_stream(W, order):
    tiles = [W[k * P:(k + 1) * P, m * P:(m + 1) * P] for (k, m) in order]
    return np.ascontiguousarray(np.concatenate(tiles, axis=1), dtype=np.float32)


def _orders():
    m_order = []
    for g in range(NC):
        m_order += [g, 8 + g]
    m_order += list(range(16, 24))
    o_inproj = [(k, m) for m in m_order for k in range(NC)]
    o_outproj = [(k, m) for m in range(NC) for k in range(NC)]
    o_fc = [(k, hm) for g2 in range(NHC // 2) for k in range(NC)
            for hm in (2 * g2, 2 * g2 + 1)]
    o_proj = [(k, m) for quad in range(2) for k in range(NHC)
              for m in range(quad * 4, quad * 4 + 4)]
    return o_inproj, o_outproj, o_fc, o_proj


def _host_layout(inp):
    o_inproj, o_outproj, o_fc, o_proj = _orders()

    w_in = np.asarray(inp["L0_attn_w_in"], np.float32)
    b_in = np.asarray(inp["L0_attn_b_in"], np.float32)
    perm = np.empty(3 * D, np.int64)
    for qkv in range(3):
        for h in range(NH):
            s = qkv * D + h * HD
            perm[s:s + HD] = np.arange(h * 192 + qkv * HD, h * 192 + qkv * HD + HD)
    w_inT_p = np.ascontiguousarray(w_in[perm, :].T)
    b_in_p = b_in[perm]

    w_outT = np.ascontiguousarray(np.asarray(inp["L0_attn_w_out"], np.float32).T)
    wfc = np.asarray(inp["L0_ffn_Wfc"], np.float32)
    wproj = np.asarray(inp["L0_ffn_Wproj"], np.float32)

    d = {}
    d["win_s"] = _tile_stream(w_inT_p, o_inproj)
    d["wout_s"] = _tile_stream(w_outT, o_outproj)
    wr = np.asarray(inp["L0_ffn_Wr"], np.float32)
    d["wr_s"] = np.ascontiguousarray(
        wr.reshape(NC, P, NE).transpose(1, 0, 2).reshape(P, NC * NE))
    d["wfcs_s"] = _tile_stream(np.asarray(inp["L0_ffn_Wfc_s"], np.float32), o_fc)
    d["wprojs_s"] = _tile_stream(np.asarray(inp["L0_ffn_Wproj_s"], np.float32), o_proj)
    d["wfc8_s"] = np.stack([_tile_stream(wfc[e], o_fc) for e in range(NE)])
    d["wproj8_s"] = np.stack([_tile_stream(wproj[e], o_proj) for e in range(NE)])

    def fm(v, n):
        return np.ascontiguousarray(v.reshape(n, P).T, dtype=np.float32)

    d["b_in_t"] = fm(b_in_p, 24)
    d["b_out_t"] = fm(np.asarray(inp["L0_attn_b_out"], np.float32), 8)
    bfc_all = [fm(np.asarray(inp["L0_ffn_bfc_s"], np.float32), 32)]
    bfc_all += [fm(np.asarray(inp["L0_ffn_bfc"], np.float32)[e], 32) for e in range(NE)]
    d["bfc_t"] = np.ascontiguousarray(np.concatenate(bfc_all, axis=1))
    d["bprojs_t"] = fm(np.asarray(inp["L0_ffn_bproj_s"], np.float32), 8)
    d["bprojT_t"] = np.ascontiguousarray(np.asarray(inp["L0_ffn_bproj"], np.float32))
    d["ident_t"] = np.eye(P, dtype=np.float32)
    selT = np.zeros((NH, NC * P), np.float32)
    for c in range(NC):
        for h in (2 * c, 2 * c + 1):
            selT[h, c * P + (h % 2) * HD: c * P + (h % 2) * HD + HD] = 1.0
    d["selT_t"] = selT
    selE = np.zeros((NE, NE * P), np.float32)
    for e in range(NE):
        selE[e, e * P:(e + 1) * P] = 1.0
    d["selE_t"] = selE
    return d


class Plan:
    SELF_SEM = {"vector": "pV", "scalar": "pS", "gpsimd": "pG"}

    def __init__(self, sems):
        self.ops = {e: [] for e in ENGINES}
        self.cnt = dict.fromkeys(sems, 0)
        self.mark = {e: dict.fromkeys(sems, 0) for e in ENGINES}

    def emit(self, eng, fn, waits=(), inc=None, amt=1):
        ss = self.SELF_SEM.get(eng)
        waits = list(waits)
        if ss is not None:
            waits.append((ss, self.cnt[ss]))
            if inc is None:
                inc = ss
            assert inc == ss, (eng, inc)
        w = []
        for (s, v) in waits:
            if v is not None and v > self.mark[eng][s]:
                w.append((s, v))
                self.mark[eng][s] = v
        ms = None
        if inc is not None:
            self.cnt[inc] += amt
            ms = self.cnt[inc]
        self.ops[eng].append((fn, tuple(w), inc, amt))
        return ms


class Banks:
    def __init__(self):
        self.freed = [None] * 8

    def acquire(self, *bs):
        return [self.freed[b] for b in bs if self.freed[b] is not None]

    def release(self, bs, sem, val):
        for b in bs:
            self.freed[b] = (sem, val)


def _build_nc():
    nc = bass.Bass("TRN2", target_bir_lowering=False)

    x_in = nc.dram_tensor("x_in", [L, D], F32, kind="ExternalInput").ap()
    win_s = nc.dram_tensor("win_s", [P, 192 * P], F32, kind="ExternalInput").ap()
    wout_s = nc.dram_tensor("wout_s", [P, 64 * P], F32, kind="ExternalInput").ap()
    wr_s = nc.dram_tensor("wr_s", [P, NC * NE], F32, kind="ExternalInput").ap()
    wfcs_s = nc.dram_tensor("wfcs_s", [P, 256 * P], F32, kind="ExternalInput").ap()
    wprojs_s = nc.dram_tensor("wprojs_s", [P, 256 * P], F32, kind="ExternalInput").ap()
    wfc8_s = nc.dram_tensor("wfc8_s", [NE, P, 256 * P], F32, kind="ExternalInput").ap()
    wproj8_s = nc.dram_tensor("wproj8_s", [NE, P, 256 * P], F32,
                              kind="ExternalInput").ap()
    b_in_t = nc.dram_tensor("b_in_t", [P, 24], F32, kind="ExternalInput").ap()
    b_out_t = nc.dram_tensor("b_out_t", [P, 8], F32, kind="ExternalInput").ap()
    bfc_t = nc.dram_tensor("bfc_t", [P, 288], F32, kind="ExternalInput").ap()
    bprojs_t = nc.dram_tensor("bprojs_t", [P, 8], F32, kind="ExternalInput").ap()
    bprojT_t = nc.dram_tensor("bprojT_t", [NE, D], F32, kind="ExternalInput").ap()
    ident_t = nc.dram_tensor("ident_t", [P, P], F32, kind="ExternalInput").ap()
    selT_t = nc.dram_tensor("selT_t", [NH, NC * P], F32, kind="ExternalInput").ap()
    selE_t = nc.dram_tensor("selE_t", [NE, NE * P], F32, kind="ExternalInput").ap()
    out_d = nc.dram_tensor("out", [L, D], F32, kind="ExternalOutput").ap()

    taps = {}
    if TAPS:
        mk = {
            "xn": ([P, NC * L], BF16), "rtm": ([P, 256], F32),
            "gate": ([P, P], F32), "gate_em": ([NE, L], F32),
            "vs": ([P, NC * L], BF16), "acc": ([P, NC * L], F32),
        }
        for t in TAPS:
            sh, dt = mk[t]
            taps[t] = nc.dram_tensor("tap_" + t, sh, dt, kind="ExternalOutput").ap()

    stack = ExitStack()
    sb = lambda n, s, dt: stack.enter_context(nc.sbuf_tensor(n, s, dt))

    acc = sb("acc", [P, NC * L], F32)
    xn = sb("xn", [P, NC * L], BF16)
    wring = [sb(f"wring{i}", [P, CHUNK_COLS], BF16) for i in range(2)]
    ipbuf = [sb(f"ipring{i}", [P, IP_COLS], BF16) for i in range(2)]
    ident = sb("ident", [P, P], F32)
    consts = sb("consts", [P, 2], F32)
    ones1f = sb("ones1f", [1, P], F32)
    onesPb = sb("onesPb", [P, 1], BF16)
    b_in_sb = sb("b_in_sb", [P, 24], F32)
    b_out_sb = sb("b_out_sb", [P, 8], F32)
    bfc_sb = sb("bfc_sb", [P, 288], F32)
    bprojs_sb = sb("bprojs_sb", [P, 8], F32)
    wr_sb = sb("wr_sb", [P, NC * NE], BF16)
    rtm = sb("rtm", [P, 256], F32)
    rts = sb("rts", [P, 256], F32)
    gtm = sb("gtm", [P, P], F32)
    gem = sb("gem", [NE, L], F32)
    t_s = sb("t_s", [P, TT], F32)
    mxT = [sb(f"mxT{i}", [P, 1], F32) for i in range(2)]
    negT = [sb(f"negT{i}", [P, 1], F32) for i in range(2)]
    sT = [sb(f"sT{i}", [P, 1], F32) for i in range(2)]
    recT = [sb(f"recT{i}", [P, 1], F32) for i in range(2)]
    r1T = [sb(f"r1T{i}", [P, 1], F32) for i in range(2)]
    r2T = [sb(f"r2T{i}", [P, 1], F32) for i in range(2)]
    glg = sb("glg", [P, NE], F32)
    gpr = sb("gpr", [P, NE], F32)
    gm8 = sb("gm8", [P, NE], F32)
    gmsk = sb("gmsk", [P, NE], F32)
    ggu = sb("ggu", [P, NE], F32)
    gg8 = sb("gg8", [P, NE], F32)
    gmxT = sb("gmxT", [P, 1], F32)
    gngT = sb("gngT", [P, 1], F32)
    gseT = sb("gseT", [P, 1], F32)
    grsT = sb("grsT", [P, 1], F32)
    gdnT = sb("gdnT", [P, 1], F32)
    grdT = sb("grdT", [P, 1], F32)
    sdum = sb("sdum", [P, 1], F32)

    early = ExitStack()
    esb = lambda n, s, dt: early.enter_context(nc.sbuf_tensor(n, s, dt))
    xstage = [esb(f"xstage{i}", [P, D], F32) for i in range(2)]
    xb = esb("xb", [P, L], BF16)
    xb2 = esb("xb2", [P, L], BF16)
    sxt = esb("sxt", [1, 512], F32)
    sx2t = esb("sx2t", [1, 512], F32)
    tmpa = esb("tmpa", [1, 512], F32)
    tmpb = esb("tmpb", [1, 512], F32)
    ab = esb("ab", [P, 2 * L], F32)
    t1 = esb("t1", [P, L], F32)

    psum = stack.enter_context(nc.psum_tensor("psarena", [P, 8 * 512], F32))

    def bank(b, n=512, p0=0, p1=P):
        return psum[p0:p1, b * 512: b * 512 + n]

    sems = ("dW0", "dW1", "dI0", "dI1", "dX0", "dX1", "dO0", "dO1", "dB", "dG",
            "pT", "pV", "pS", "pG")
    semh = {s: stack.enter_context(nc.semaphore(s)) for s in sems}
    pl = Plan(sems)
    bk = Banks()

    def dma(eng, fn, sem, waits=()):
        waits = list(waits) + [(sem, pl.cnt[sem])]
        w = []
        for (s, v) in waits:
            if v is not None and v > pl.mark[eng][s]:
                w.append((s, v))
                pl.mark[eng][s] = v
        pl.cnt[sem] += 16
        pl.ops[eng].append((fn, tuple(w), sem, 16))
        return pl.cnt[sem]

    class Ring:
        def __init__(self, slots, sembase, seq):
            self.slots = slots
            self.sems = sembase
            self.seq = seq
            self.issued = 0
            self.load = [None] * len(seq)
            self.rel = [None] * len(seq)

        def issue(self):
            c = self.issued
            waits = []
            if c >= 2:
                rel = self.rel[c - 2]
                assert rel is not None, f"ring chunk {c - 2} not released"
                waits.append(("pT", rel))
            slot = self.slots[c % 2]
            src = self.seq[c]
            self.load[c] = dma(
                "gpsimd", lambda g, s=slot, a=src: g.dma_start(out=s[:, :], in_=a),
                self.sems[c % 2], waits)
            self.issued += 1

        def use(self, c):
            while self.issued <= min(c + 1, len(self.seq) - 1):
                self.issue()
            return (self.sems[c % 2], self.load[c])

        def tile(self, c, i):
            return self.slots[c % 2][:, i * P:(i + 1) * P]

        def done(self, c):
            self.rel[c] = pl.cnt["pT"]

    mseq = []
    ffn_chunk0 = 0
    for t in range(NTT):
        for e in range(NE + 1):
            for i in range(8):
                if e == 0:
                    mseq.append(wfcs_s[:, i * CHUNK_COLS:(i + 1) * CHUNK_COLS])
                else:
                    mseq.append(wfc8_s[e - 1, :, i * CHUNK_COLS:(i + 1) * CHUNK_COLS])
            for i in range(8):
                if e == 0:
                    mseq.append(wprojs_s[:, i * CHUNK_COLS:(i + 1) * CHUNK_COLS])
                else:
                    mseq.append(wproj8_s[e - 1, :, i * CHUNK_COLS:(i + 1) * CHUNK_COLS])
    vin_chunk0 = len(mseq)
    for i in range(2):
        mseq.append(win_s[:, (128 + i * 32) * P:(128 + (i + 1) * 32) * P])
    op_chunk0 = len(mseq)
    for i in range(2):
        mseq.append(wout_s[:, i * CHUNK_COLS:(i + 1) * CHUNK_COLS])
    mring = Ring(wring, ("dW0", "dW1"), mseq)

    ipseq = [win_s[:, g * IP_COLS:(g + 1) * IP_COLS] for g in range(NC)]
    ipring = Ring(ipbuf, ("dI0", "dI1"), ipseq)

    ms_ident = dma("sync", lambda sy: sy.dma_start(
        out=ident[:, :], in_=ident_t[:, :]), "dB")
    pl.emit("gpsimd", lambda g: g.memset(ones1f[:, :], 1.0))
    pl.emit("gpsimd", lambda g: g.memset(onesPb[:, :], 1.0))
    pl.emit("gpsimd", lambda g: g.memset(consts[:, 0:1], 1e-5))
    pl.emit("gpsimd", lambda g: g.memset(consts[:, 1:2], SC))
    ms_setup = pl.emit("gpsimd", lambda g: g.memset(sdum[:, :], 0.0), inc="pG")
    ms_wr = dma("gpsimd", lambda g: g.dma_start(
        out=wr_sb[:, :], in_=wr_s[:, :]), "dG")
    for (dst_, src_) in ((b_in_sb, b_in_t), (b_out_sb, b_out_t), (bfc_sb, bfc_t),
                         (bprojs_sb, bprojs_t)):
        dma("sync", lambda sy, dd=dst_, ss=src_: sy.dma_start(
            out=dd[:, :], in_=ss[:, :]), "dB")
    ms_bias = pl.cnt["dB"]

    tr_banks = (0, 1)
    ms_xT = {}
    for tc in range(NTC):
        slot = xstage[tc % 2]
        waits = [("pT", ms_xT[tc - 2])] if tc >= 2 else []
        ms_ld = dma("sync", lambda sy, s=slot, tc=tc: sy.dma_start(
            out=s[:, :], in_=x_in[tc * P:(tc + 1) * P, :]),
            f"dX{tc % 2}", waits)
        last = None
        for c in range(NC):
            b = tr_banks[c % 2]
            w = [(f"dX{tc % 2}", ms_ld), ("dB", ms_ident),
                 ("pG", ms_setup)] + bk.acquire(b)
            last = pl.emit("tensor", lambda t, s=slot, c=c, b=b: t.transpose(
                bank(b, P), s[:, c * P:(c + 1) * P], ident[:, :]), w, inc="pT")
            msv = pl.emit("vector", lambda v, c=c, tc=tc, b=b: v.tensor_copy(
                acc[:, c * L + tc * P: c * L + (tc + 1) * P], bank(b, P)),
                [("pT", last)], inc="pV")
            bk.release([b], "pV", msv)
        ms_xT[tc] = last
    ms_acc_ready = pl.cnt["pV"]

    stat_w = bk.acquire(*range(8))
    ms_stat_mm = None
    for c in range(NC):
        wcp = [("pV", ms_acc_ready)] + ([("pT", ms_stat_mm)] if c > 0 else [])
        pl.emit("vector", lambda v, c=c: v.tensor_copy(
            xb[:, :], acc[:, c * L:(c + 1) * L]), wcp, inc="pV")
        ms2 = pl.emit("vector", lambda v: v.tensor_mul(
            xb2[:, :], xb[:, :], xb[:, :]), inc="pV")
        for tt in range(NTT):
            w = [("pV", ms2)] + (stat_w if c == 0 else [])
            pl.emit("tensor", lambda t, tt=tt, c=c: t.matmul(
                psum[0:1, tt * 512:(tt + 1) * 512], onesPb[:, :],
                xb[:, tt * 512:(tt + 1) * 512],
                start=(c == 0), stop=(c == NC - 1)), w)
            ms_stat_mm = pl.emit("tensor", lambda t, tt=tt, c=c: t.matmul(
                psum[0:1, (4 + tt) * 512:(5 + tt) * 512], onesPb[:, :],
                xb2[:, tt * 512:(tt + 1) * 512],
                start=(c == 0), stop=(c == NC - 1)), inc="pT")
    for tt in range(NTT):
        pl.emit("vector", lambda v, tt=tt: v.tensor_copy(
            sxt[:, :], psum[0:1, tt * 512:(tt + 1) * 512]),
            [("pT", ms_stat_mm)], inc="pV")
        ms_ev2 = pl.emit("vector", lambda v, tt=tt: v.tensor_copy(
            sx2t[:, :], psum[0:1, (4 + tt) * 512:(5 + tt) * 512]), inc="pV")
        bk.release([tt, 4 + tt], "pV", ms_ev2)
        pl.emit("vector", lambda v: v.tensor_scalar_mul(sxt[:, :], sxt[:, :], 1.0 / D))
        pl.emit("vector", lambda v: v.tensor_scalar_mul(sx2t[:, :], sx2t[:, :],
                                                        1.0 / D))
        pl.emit("vector", lambda v: v.tensor_mul(tmpa[:, :], sxt[:, :], sxt[:, :]))
        ms_var = pl.emit("vector", lambda v: v.tensor_sub(
            sx2t[:, :], sx2t[:, :], tmpa[:, :]), inc="pV")
        ms_std = pl.emit("scalar", lambda s: s.activation(
            tmpa[:, :], sx2t[:, :], AF.Sqrt, bias=consts[0:1, 0:1]),
            [("pV", ms_var), ("pG", ms_setup)], inc="pS")
        pl.emit("vector", lambda v: v.reciprocal(sx2t[:, :], tmpa[:, :]),
                [("pS", ms_std)])
        pl.emit("vector", lambda v: v.tensor_mul(tmpb[:, :], sxt[:, :], sx2t[:, :]))
        ms_bv = pl.emit("vector", lambda v: v.tensor_scalar_mul(
            tmpb[:, :], tmpb[:, :], -1.0), inc="pV")
        for half in range(2):
            b = (tt, 4 + tt)[half]
            srow = sx2t if half == 0 else tmpb
            w = [("pV", ms_bv)] + bk.acquire(b)
            msmm = pl.emit("tensor", lambda t, b=b, s=srow: t.matmul(
                bank(b), ones1f[:, :], s[0:1, :], start=True, stop=True),
                w, inc="pT")
            msv = pl.emit("vector", lambda v, half=half, tt=tt, b=b: v.tensor_copy(
                ab[:, half * L + tt * 512: half * L + tt * 512 + 512], bank(b)),
                [("pT", msmm)], inc="pV")
            bk.release([b], "pV", msv)
    ms_ab = pl.cnt["pV"]

    for c in range(NC):
        pl.emit("vector", lambda v, c=c: v.tensor_mul(
            t1[:, :], acc[:, c * L:(c + 1) * L], ab[:, 0:L]), [("pV", ms_ab)])
        pl.emit("vector", lambda v, c=c: v.tensor_add(
            xn[:, c * L:(c + 1) * L], t1[:, :], ab[:, L:2 * L]), inc="pV")
    ms_xn_all = pl.cnt["pV"]
    if TAPS and "xn" in taps:
        dma("gpsimd", lambda g: g.dma_start(out=taps["xn"][:, :], in_=xn[:, :]),
            "dG", [("pV", ms_xn_all)])

    rtr_banks = (6, 7)
    ms_exp_g = None
    for tc in range(NTC):
        b = rtr_banks[tc % 2]
        w = [("pV", ms_xn_all), ("dG", ms_wr)] + bk.acquire(b)
        last = None
        for k in range(NC):
            last = pl.emit("tensor", lambda t, tc=tc, k=k, b=b: t.matmul(
                bank(b, NE), xn[:, k * L + tc * P: k * L + (tc + 1) * P],
                wr_sb[:, k * NE:(k + 1) * NE],
                start=(k == 0), stop=(k == NC - 1)),
                w if k == 0 else (), inc=("pT" if k == NC - 1 else None))
        ms_lg = pl.emit("vector", lambda v, b=b: v.tensor_copy(
            glg[:, :], bank(b, NE)), [("pT", last)], inc="pV")
        bk.release([b], "pV", ms_lg)
        wmax = [("pS", ms_exp_g)] if ms_exp_g else []
        pl.emit("vector", lambda v: v.reduce_max(gmxT[:, 0:1], glg[:, :],
                                                 axis=AX.X), wmax)
        ms_ng = pl.emit("vector", lambda v: v.tensor_scalar_mul(
            gngT[:, 0:1], gmxT[:, 0:1], -1.0), inc="pV")
        ms_exp_g = pl.emit("scalar", lambda s: s.activation(
            gpr[:, :], glg[:, :], AF.Exp, bias=gngT[:, 0:1],
            accum_out=gseT[:, 0:1]), [("pV", ms_ng)], inc="pS")
        pl.emit("vector", lambda v: v.reciprocal(grsT[:, 0:1], gseT[:, 0:1]),
                [("pS", ms_exp_g)])
        pl.emit("vector", lambda v: v.tensor_scalar_mul(gpr[:, :], gpr[:, :],
                                                        grsT[:, 0:1]))
        pl.emit("vector", lambda v: v.max(out=gm8[:, :], in_=gpr[:, :]))
        pl.emit("vector", lambda v: v.tensor_single_scalar(
            out=gmsk[:, :], in_=gpr[:, :], scalar=gm8[:, 1:2], op=ALU.is_ge))
        pl.emit("vector", lambda v: v.tensor_mul(ggu[:, :], gpr[:, :], gmsk[:, :]))
        pl.emit("vector", lambda v: v.tensor_add(gdnT[:, 0:1], gm8[:, 0:1],
                                                 gm8[:, 1:2]))
        pl.emit("vector", lambda v: v.reciprocal(grdT[:, 0:1], gdnT[:, 0:1]))
        pl.emit("vector", lambda v: v.tensor_scalar_mul(
            gg8[:, :], ggu[:, :], grdT[:, 0:1]))
        for e in range(NE):
            ms_g = pl.emit("vector", lambda v, tc=tc, e=e: v.tensor_copy(
                gtm[:, e * NTC + tc: e * NTC + tc + 1], gg8[:, e:e + 1]),
                inc="pV")
    ms_gate_tm = ms_g
    if TAPS and "gate" in taps:
        dma("sync", lambda sy: sy.dma_start(out=taps["gate"][:, :], in_=gtm[:, :]),
            "dB", [("pV", ms_gate_tm)])
    bT = 6
    w = [("pV", ms_gate_tm)] + bk.acquire(bT)
    ms_gT = pl.emit("tensor", lambda t: t.transpose(
        bank(bT, P), gtm[:, :], ident[:, :]), w, inc="pT")
    ms_gTc = pl.emit("vector", lambda v: v.tensor_copy(rts[:, 0:P], bank(bT, P)),
                     [("pT", ms_gT)], inc="pV")
    bk.release([bT], "pV", ms_gTc)
    for e in range(NE):
        dma("sync", lambda sy, e=e: sy.dma_start(
            out=gem[e:e + 1, :], in_=rts[e * NTC:(e + 1) * NTC, 0:P]),
            "dB", [("pV", ms_gTc)])
    ms_gem_all = pl.cnt["dB"]
    if TAPS and "gate_em" in taps:
        dma("sync", lambda sy: sy.dma_start(out=taps["gate_em"][:, :], in_=gem[:, :]),
            "dB", [("dB", ms_gem_all)])

    early.close()
    stageB = ExitStack()
    bsb = lambda n, s, dt: stageB.enter_context(nc.sbuf_tensor(n, s, dt))
    ebuf = bsb("ebuf", [P, L], F32)
    qg = bsb("qg", [P, 2 * L], BF16)
    kg = bsb("kg", [P, 2 * L], BF16)
    stageC = ExitStack()
    csb = lambda n, s, dt: stageC.enter_context(nc.sbuf_tensor(n, s, dt))
    bprojT_sb = csb("bprojT_sb", [NE, D], F32)
    selE = csb("selE", [NE, NE * P], F32)
    g_sb = csb("g_sb", [P, TT], F32)
    t_v = csb("t_v", [P, TT], F32)
    hbuf = csb("hbuf", [P, NHC * TT], BF16)
    ms_bpT = dma("sync", lambda sy: sy.dma_start(
        out=bprojT_sb[:, :], in_=bprojT_t[:, :]), "dB")
    ms_selE = dma("sync", lambda sy: sy.dma_start(
        out=selE[:, :], in_=selE_t[:, :]), "dB")

    scores_ap = psum[:, 4 * 512: 8 * 512]
    prev_a1 = None
    prev_s = [None, None]
    ms_ip_done = [None] * NC
    ms_qk_last = [None] * NC
    unit_ctr = [0]

    def emit_burst(g):
        wready = ipring.use(g)
        waits_war = []
        if g >= 2 and ms_qk_last[g - 2] is not None:
            waits_war.append(("pT", ms_qk_last[g - 2]))
        col0 = (g % 2) * L
        for j in range(2):
            dstbuf = qg if j == 0 else kg
            bias_col = g if j == 0 else 8 + g
            w0 = [wready, ("pV", ms_xn_all), ("dB", ms_bias)] \
                + bk.acquire(4, 5, 6, 7) + waits_war
            lasts = [None] * NTT
            for k in range(NC):
                for tt in range(NTT):
                    ti = j * 8 + k
                    lasts[tt] = pl.emit(
                        "tensor", lambda t, g=g, ti=ti, k=k, tt=tt: t.matmul(
                            bank(4 + tt), ipring.tile(g, ti),
                            xn[:, k * L + tt * 512: k * L + tt * 512 + 512],
                            start=(k == 0), stop=(k == NC - 1)),
                        w0 if (k == 0 and tt == 0) else (),
                        inc=("pT" if k == NC - 1 else None))
            for tt in range(NTT):
                if j == 0:
                    msev = pl.emit("vector",
                                   lambda v, tt=tt, col0=col0, dstbuf=dstbuf,
                                   bc=bias_col: v.tensor_scalar_add(
                                       dstbuf[:, col0 + tt * 512:
                                              col0 + tt * 512 + 512],
                                       bank(4 + tt), b_in_sb[:, bc:bc + 1]),
                                   [("pT", lasts[tt])], inc="pV")
                else:
                    msev = pl.emit("vector",
                                   lambda v, tt=tt, col0=col0, dstbuf=dstbuf,
                                   bc=bias_col: v.tensor_scalar(
                                       out=dstbuf[:, col0 + tt * 512:
                                                  col0 + tt * 512 + 512],
                                       in0=bank(4 + tt),
                                       scalar1=b_in_sb[:, bc:bc + 1],
                                       scalar2=SC, op0=ALU.add, op1=ALU.mult),
                                   [("pT", lasts[tt])], inc="pV")
                bk.release([4 + tt], "pV", msev)
        ms_ip_done[g] = pl.cnt["pV"]
        ipring.done(g)

    def emit_qk_unit(g, hh, lc):
        nonlocal prev_a1
        u = unit_ctr[0] % 2
        unit_ctr[0] += 1
        pb = hh * HD
        col0 = (g % 2) * L
        w = [("pV", ms_ip_done[g])] + bk.acquire(4, 5, 6, 7)
        last = None
        for mt in range(4):
            last = pl.emit("tensor", lambda t, pb=pb, col0=col0, lc=lc, mt=mt:
                           t.matmul(bank(4 + mt),
                                    qg[pb:pb + HD,
                                       col0 + lc * P: col0 + (lc + 1) * P],
                                    kg[pb:pb + HD,
                                       col0 + mt * 512: col0 + mt * 512 + 512],
                                    start=True, stop=True),
                           w if mt == 0 else (),
                           inc=("pT" if mt == 3 else None))
        ms_qk_last[g] = last
        wv = [("pT", last)]
        if prev_s[u] is not None:
            wv.append(("pS", prev_s[u]))
        pl.emit("vector", lambda v, u=u: v.reduce_max(
            mxT[u][:, 0:1], scores_ap, axis=AX.X), wv)
        ms_ng = pl.emit("vector", lambda v, u=u: v.tensor_scalar_mul(
            negT[u][:, 0:1], mxT[u][:, 0:1], -1.0), inc="pV")
        we = [("pV", ms_ng)]
        if prev_a1 is not None:
            we.append(("pV", prev_a1))
        ms_exp = pl.emit("scalar", lambda s, u=u: s.activation(
            ebuf[:, :], scores_ap, AF.Exp, bias=negT[u][:, 0:1],
            accum_out=sT[u][:, 0:1]), we, inc="pS")
        bk.release([4, 5, 6, 7], "pS", ms_exp)
        ms_rc = pl.emit("vector", lambda v, u=u: v.reciprocal(
            recT[u][:, 0:1], sT[u][:, 0:1]), [("pS", ms_exp)], inc="pV")
        prev_a1 = pl.emit("vector", lambda v, u=u: v.tensor_scalar(
            out=ebuf[:, 0:1024], in0=ebuf[:, 0:1024], scalar1=recT[u][:, 0:1],
            scalar2=0.0, op0=ALU.mult, op1=ALU.add,
            accum_out=r1T[u][:, 0:1]), inc="pV")
        ms_a2 = pl.emit("scalar", lambda s, u=u: s.activation(
            ebuf[:, 1024:2048], ebuf[:, 1024:2048], AF.Copy,
            scale=recT[u][:, 0:1], accum_out=r2T[u][:, 0:1]),
            [("pV", ms_rc)], inc="pS")
        prev_s[u] = ms_a2
        h = 2 * g + hh
        pl.emit("vector", lambda v, u=u, h=h, lc=lc: v.tensor_add(
            rtm[:, h * NTC + lc: h * NTC + lc + 1], r1T[u][:, 0:1],
            r2T[u][:, 0:1]), [("pS", ms_a2)], inc="pV")

    burst_at = {0: [0, 1], 4: [2], 8: [3], 12: [4], 16: [5], 20: [6], 24: [7]}
    qk_queue = []

    fc_pairs = ((0, 1), (2, 3))
    ms_h_prev_read = None
    for t in range(NTT):
        for e in range(NE + 1):
            unit = t * (NE + 1) + e
            for g in burst_at.get(unit, []):
                emit_burst(g)
                qk_queue.extend((g, hh, lc) for hh in range(2)
                                for lc in range(NTC))
            c0 = ffn_chunk0 + unit * 16
            msgc = None
            if e > 0:
                bG = 4
                w = [("dB", ms_gem_all), ("dB", ms_selE)] + bk.acquire(bG)
                msg = pl.emit("tensor", lambda tn, e=e, t=t: tn.matmul(
                    bank(bG), selE[:, (e - 1) * P: e * P],
                    gem[:, t * 512:(t + 1) * 512],
                    start=True, stop=True), w, inc="pT")
                msgc = pl.emit("vector", lambda v: v.tensor_copy(
                    g_sb[:, :], bank(bG)), [("pT", msg)], inc="pV")
                bk.release([bG], "pV", msgc)
            ms_gelu_half = [None, None]
            for g2 in range(NHC // 2):
                cc = c0 + g2 // 2
                wready = mring.use(cc)
                base_ti = (g2 % 2) * 16
                pair = fc_pairs[g2 % 2]
                for j, hm in enumerate((2 * g2, 2 * g2 + 1)):
                    b = pair[j]
                    w = [wready, ("pV", ms_xn_all)] + bk.acquire(b)
                    if ms_h_prev_read is not None:
                        w.append(("pT", ms_h_prev_read))
                    last = None
                    for k in range(NC):
                        ti = base_ti + k * 2 + j
                        last = pl.emit("tensor",
                                       lambda tn, cc=cc, ti=ti, k=k, t=t, b=b:
                                       tn.matmul(bank(b), mring.tile(cc, ti),
                                                 xn[:, k * L + t * 512:
                                                    k * L + t * 512 + 512],
                                                 start=(k == 0),
                                                 stop=(k == NC - 1)),
                                       w if k == 0 else (),
                                       inc=("pT" if k == NC - 1 else None))
                    msev = pl.emit("vector", lambda v, b=b, hm=hm, e=e:
                                   v.tensor_scalar_add(
                                       hbuf[:, hm * 512:(hm + 1) * 512], bank(b),
                                       bfc_sb[:, e * 32 + hm: e * 32 + hm + 1]),
                                   [("pT", last)], inc="pV")
                    bk.release([b], "pV", msev)
                if g2 == NHC // 4 - 1 or g2 == NHC // 2 - 1:
                    half = 0 if g2 == NHC // 4 - 1 else 1
                    ms_gelu_half[half] = pl.emit(
                        "scalar", lambda s, half=half: s.activation(
                            hbuf[:, half * 8192: (half + 1) * 8192],
                            hbuf[:, half * 8192: (half + 1) * 8192],
                            AF.Gelu_apprx_tanh),
                        [("pV", msev)], inc="pS")
                if g2 % 2 == 1:
                    mring.done(cc)
                    if qk_queue:
                        emit_qk_unit(*qk_queue.pop(0))
            h_read_last = None
            for quad in range(2):
                qb = [4, 5, 6, 7]
                seeded = (e == 0)
                lasts = [None] * 4
                for k in range(NHC):
                    cck = c0 + 8 + (quad * NHC * 4 + k * 4) // CHUNK_TILES
                    wready = mring.use(cck)
                    for i in range(4):
                        m = quad * 4 + i
                        b = qb[i]
                        ti = (k * 4 + i) % CHUNK_TILES
                        if k == 0:
                            w = [wready, ("pS", ms_gelu_half[0])] + bk.acquire(b)
                            if seeded:
                                pl.emit("tensor", lambda tn, m=m, t=t, b=b:
                                        tn.matmul(bank(b),
                                                  bprojT_sb[:, m * P:(m + 1) * P],
                                                  gem[:, t * 512:(t + 1) * 512],
                                                  start=True, stop=False),
                                        w + [("dB", max(ms_gem_all, ms_bpT))])
                                w = []
                        else:
                            w = [wready]
                            if k == NHC // 2 and i == 0:
                                w.append(("pS", ms_gelu_half[1]))
                        lasts[i] = pl.emit(
                            "tensor",
                            lambda tn, cck=cck, ti=ti, k=k, b=b, seeded=seeded:
                            tn.matmul(bank(b), mring.tile(cck, ti),
                                      hbuf[:, k * 512:(k + 1) * 512],
                                      start=(k == 0 and not seeded),
                                      stop=(k == NHC - 1)),
                            w, inc=("pT" if (k == NHC - 1 or
                                             (k % 8 == 7 and i == 3)) else None))
                    if k % 8 == 7:
                        mring.done(c0 + 8 + quad * 4 + k // 8)
                h_read_last = lasts[3]
                for i in range(4):
                    m = quad * 4 + i
                    b = qb[i]
                    if e == 0:
                        msad = pl.emit(
                            "vector", lambda v, b=b, m=m, t=t: v.scalar_tensor_tensor(
                                acc[:, m * L + t * 512: m * L + t * 512 + 512],
                                bank(b), bprojs_sb[:, m:m + 1],
                                acc[:, m * L + t * 512: m * L + t * 512 + 512],
                                ALU.add, ALU.add),
                            [("pT", lasts[i])], inc="pV")
                    else:
                        pl.emit("vector", lambda v, b=b: v.tensor_tensor(
                            t_v[:, :], g_sb[:, :], bank(b), ALU.mult),
                            [("pT", lasts[i]), ("pV", msgc)], inc="pV")
                        msad = pl.emit("vector", lambda v, m=m, t=t: v.tensor_add(
                            acc[:, m * L + t * 512: m * L + t * 512 + 512],
                            acc[:, m * L + t * 512: m * L + t * 512 + 512],
                            t_v[:, :]), inc="pV")
                    bk.release([b], "pV", msad)
            ms_h_prev_read = h_read_last

    while qk_queue:
        emit_qk_unit(*qk_queue.pop(0))
    ms_r_all = pl.cnt["pV"]

    stageC.close()
    stageD = ExitStack()
    dsb = lambda n, s, dt: stageD.enter_context(nc.sbuf_tensor(n, s, dt))
    vbuf = dsb("vbuf", [P, NC * L], BF16)
    selT = dsb("selT", [NH, NC * P], F32)
    rh = dsb("rh", [NH, L], F32)
    ms_selT = dma("sync", lambda sy: sy.dma_start(
        out=selT[:, :], in_=selT_t[:, :]), "dB",
        [("pT", pl.cnt["pT"]), ("pS", pl.cnt["pS"]), ("pV", pl.cnt["pV"])])

    ms_v_all = None
    for j in range(NC):
        c = vin_chunk0 + j // 4
        wready = mring.use(c)
        bs = 0 if j % 2 == 0 else 4
        w0 = [wready, ("pV", ms_xn_all), ("dB", ms_bias)] \
            + bk.acquire(bs, bs + 1, bs + 2, bs + 3)
        lasts = [None] * NTT
        for k in range(NC):
            for tt in range(NTT):
                ti = (j % 4) * 8 + k
                lasts[tt] = pl.emit("tensor", lambda t, c=c, ti=ti, k=k, tt=tt,
                                    bs=bs: t.matmul(
                                        bank(bs + tt), mring.tile(c, ti),
                                        xn[:, k * L + tt * 512:
                                           k * L + tt * 512 + 512],
                                        start=(k == 0), stop=(k == NC - 1)),
                                    w0 if (k == 0 and tt == 0) else (),
                                    inc=("pT" if k == NC - 1 else None))
        for tt in range(NTT):
            ms_v_all = pl.emit("vector", lambda v, j=j, tt=tt, bs=bs:
                               v.tensor_scalar_add(
                                   vbuf[:, j * L + tt * 512:
                                        j * L + tt * 512 + 512],
                                   bank(bs + tt), b_in_sb[:, 16 + j:17 + j]),
                               [("pT", lasts[tt])], inc="pV")
            bk.release([bs + tt], "pV", ms_v_all)
        if j == 3:
            mring.done(vin_chunk0)
    mring.done(vin_chunk0 + 1)

    ms_rts = None
    for half in range(2):
        b = half % 2
        w = [("pV", ms_r_all)] + bk.acquire(b)
        msT = pl.emit("tensor", lambda t, half=half, b=b: t.transpose(
            bank(b, P), rtm[:, half * P:(half + 1) * P], ident[:, :]), w, inc="pT")
        ms_rts = pl.emit("vector", lambda v, half=half, b=b: v.tensor_copy(
            rts[:, half * P:(half + 1) * P], bank(b, P)),
            [("pT", msT), ("dB", ms_gem_all)], inc="pV")
        bk.release([b], "pV", ms_rts)
    for h in range(NH):
        half, row0 = (0, h * NTC) if h < NH // 2 else (1, (h - NH // 2) * NTC)
        dma("sync", lambda sy, h=h, half=half, row0=row0: sy.dma_start(
            out=rh[h:h + 1, :],
            in_=rts[row0:row0 + NTC, half * P:(half + 1) * P]),
            "dB", [("pV", ms_rts)])
    ms_rh = pl.cnt["dB"]
    if TAPS and "rtm" in taps:
        dma("sync", lambda sy: sy.dma_start(out=taps["rtm"][:, :], in_=rtm[:, :]),
            "dB", [("pV", ms_r_all)])

    vs_banks = (2, 3)
    ms_vs_all = None
    for c in range(NC):
        for tt in range(NTT):
            b = vs_banks[(c * NTT + tt) % 2]
            w = [("dB", ms_rh), ("dB", ms_selT), ("pV", ms_v_all)] + bk.acquire(b)
            msb = pl.emit("tensor", lambda t, c=c, tt=tt, b=b: t.matmul(
                bank(b), selT[:, c * P:(c + 1) * P],
                rh[:, tt * 512:(tt + 1) * 512], start=True, stop=True),
                w, inc="pT")
            ms_vs_all = pl.emit("vector", lambda v, c=c, tt=tt, b=b: v.tensor_tensor(
                vbuf[:, c * L + tt * 512: c * L + tt * 512 + 512],
                vbuf[:, c * L + tt * 512: c * L + tt * 512 + 512],
                bank(b), ALU.mult), [("pT", msb)], inc="pV")
            bk.release([b], "pV", ms_vs_all)
    if TAPS and "vs" in taps:
        dma("gpsimd", lambda g: g.dma_start(out=taps["vs"][:, :], in_=vbuf[:, :]),
            "dG", [("pV", ms_vs_all)])

    for m in range(NC):
        c = op_chunk0 + m // 4
        wready = mring.use(c)
        bs = 0 if m % 2 == 0 else 4
        w0 = [wready, ("pV", ms_vs_all)] + bk.acquire(bs, bs + 1, bs + 2, bs + 3)
        lasts = [None] * NTT
        for k in range(NC):
            for tt in range(NTT):
                ti = (m % 4) * 8 + k
                lasts[tt] = pl.emit("tensor", lambda t, c=c, ti=ti, k=k, tt=tt,
                                    bs=bs: t.matmul(
                                        bank(bs + tt), mring.tile(c, ti),
                                        vbuf[:, k * L + tt * 512:
                                             k * L + tt * 512 + 512],
                                        start=(k == 0), stop=(k == NC - 1)),
                                    w0 if (k == 0 and tt == 0) else (),
                                    inc=("pT" if k == NC - 1 else None))
        for tt in range(NTT):
            msad = pl.emit("vector", lambda v, m=m, tt=tt, bs=bs:
                           v.scalar_tensor_tensor(
                               acc[:, m * L + tt * 512: m * L + tt * 512 + 512],
                               bank(bs + tt), b_out_sb[:, m:m + 1],
                               acc[:, m * L + tt * 512: m * L + tt * 512 + 512],
                               ALU.add, ALU.add),
                           [("pT", lasts[tt])], inc="pV")
            bk.release([bs + tt], "pV", msad)
        if m == 3:
            mring.done(op_chunk0)
    mring.done(op_chunk0 + 1)
    assert mring.issued == len(mseq), (mring.issued, len(mseq))
    assert ipring.issued == len(ipseq)

    ms_acc_final = pl.cnt["pV"]
    if TAPS and "acc" in taps:
        dma("sync", lambda sy: sy.dma_start(out=taps["acc"][:, :], in_=acc[:, :]),
            "dB", [("pV", ms_acc_final)])

    stageD.close()
    stageB.close()
    ostage = [sb(f"ostage{i}", [P, D], F32) for i in range(2)]
    fin_banks = (0, 1)
    ms_store = {}
    for tc in range(NTC):
        slot = ostage[tc % 2]
        msc = None
        for c in range(NC):
            b = fin_banks[c % 2]
            w = [("pV", ms_acc_final)] + bk.acquire(b)
            msT = pl.emit("tensor", lambda t, c=c, tc=tc, b=b: t.transpose(
                bank(b, P), acc[:, c * L + tc * P: c * L + (tc + 1) * P],
                ident[:, :]), w, inc="pT")
            wv = [("pT", msT)]
            if tc >= 2:
                wv.append((f"dO{tc % 2}", ms_store[tc - 2]))
            msc = pl.emit("vector", lambda v, s=slot, c=c, b=b: v.tensor_copy(
                s[:, c * P:(c + 1) * P], bank(b, P)), wv, inc="pV")
            bk.release([b], "pV", msc)
        ms_store[tc] = dma("sync", lambda sy, s=slot, tc=tc: sy.dma_start(
            out=out_d[tc * P:(tc + 1) * P, :], in_=s[:, :]),
            f"dO{tc % 2}", [("pV", msc)])

    pl.emit("sync", lambda sy: sy.nop(),
            [(s, pl.cnt[s]) for s in
             ("dW0", "dW1", "dI0", "dI1", "dX0", "dX1", "dO0", "dO1", "dB", "dG")])

    with nc.Block() as block:
        for engname in ENGINES:
            def _emit(eng, _ops=pl.ops[engname]):
                for fn, waits, inc, amt in _ops:
                    for (s, v) in waits:
                        eng.wait_ge(semh[s], v)
                    instr = fn(eng)
                    if inc is not None:
                        instr.then_inc(semh[inc], amt)
            getattr(block, engname)(_emit)

    stack.close()
    nc.finalize()
    return nc


def kernel(**inputs):
    if "nc" not in _CACHE:
        _CACHE["nc"] = _build_nc()
    nc = _CACHE["nc"]
    d = _host_layout(inputs)
    x = np.ascontiguousarray(np.asarray(inputs["L0_attn_x"], np.float32))
    in_maps = []
    for b in range(8):
        m = dict(d)
        m["x_in"] = np.ascontiguousarray(x[b])
        in_maps.append(m)
    res = run_bass_kernel_spmd(nc, in_maps, core_ids=list(range(8)))
    out = np.stack([res.results[b]["out"] for b in range(8)], axis=0)
    if TAPS:
        kernel.tap_results = res.results
    return out


# version 04 (final)
# speedup vs baseline: 1.0890x; 1.0890x over previous
import numpy as np
from contextlib import ExitStack

import concourse.bass as bass
import concourse.mybir as mybir
from concourse.bass_utils import run_bass_kernel_spmd

F32 = mybir.dt.float32
BF16 = mybir.dt.bfloat16
AF = mybir.ActivationFunctionType
ALU = mybir.AluOpType
AX = mybir.AxisListType

P = 128
L = 2048
D = 1024
NH = 16
HD = 64
FF = 4096
NE = 8
NC = D // P
NTC = L // P
TT = 512
NTT = L // TT
NHC = FF // P
CHUNK_TILES = 32
CHUNK_COLS = CHUNK_TILES * P
IP_TILES = 16
IP_COLS = IP_TILES * P
SC = 0.125

ENGINES = ("tensor", "vector", "scalar", "gpsimd", "sync")
TAPS = None
_CACHE = {}


def _tile_stream(W, order):
    tiles = [W[k * P:(k + 1) * P, m * P:(m + 1) * P] for (k, m) in order]
    return np.ascontiguousarray(np.concatenate(tiles, axis=1), dtype=np.float32)


def _orders():
    m_order = []
    for g in range(NC):
        m_order += [g, 8 + g]
    m_order += list(range(16, 24))
    o_inproj = [(k, m) for m in m_order for k in range(NC)]
    o_outproj = [(k, m) for m in range(NC) for k in range(NC)]
    o_fc = [(k, hm) for g2 in range(NHC // 2) for k in range(NC)
            for hm in (2 * g2, 2 * g2 + 1)]
    o_proj = [(k, m) for quad in range(2) for k in range(NHC)
              for m in range(quad * 4, quad * 4 + 4)]
    return o_inproj, o_outproj, o_fc, o_proj


def _host_layout(inp):
    o_inproj, o_outproj, o_fc, o_proj = _orders()

    w_in = np.asarray(inp["L0_attn_w_in"], np.float32)
    b_in = np.asarray(inp["L0_attn_b_in"], np.float32)
    perm = np.empty(3 * D, np.int64)
    for qkv in range(3):
        for h in range(NH):
            s = qkv * D + h * HD
            perm[s:s + HD] = np.arange(h * 192 + qkv * HD, h * 192 + qkv * HD + HD)
    w_inT_p = np.ascontiguousarray(w_in[perm, :].T)
    b_in_p = b_in[perm]

    w_outT = np.ascontiguousarray(np.asarray(inp["L0_attn_w_out"], np.float32).T)
    wfc = np.asarray(inp["L0_ffn_Wfc"], np.float32)
    wproj = np.asarray(inp["L0_ffn_Wproj"], np.float32)

    d = {}
    d["win_s"] = _tile_stream(w_inT_p, o_inproj)
    d["wout_s"] = _tile_stream(w_outT, o_outproj)
    wr = np.asarray(inp["L0_ffn_Wr"], np.float32)
    d["wr_s"] = np.ascontiguousarray(
        wr.reshape(NC, P, NE).transpose(1, 0, 2).reshape(P, NC * NE))
    d["wfcs_s"] = _tile_stream(np.asarray(inp["L0_ffn_Wfc_s"], np.float32), o_fc)
    d["wprojs_s"] = _tile_stream(np.asarray(inp["L0_ffn_Wproj_s"], np.float32), o_proj)
    d["wfc8_s"] = np.stack([_tile_stream(wfc[e], o_fc) for e in range(NE)])
    d["wproj8_s"] = np.stack([_tile_stream(wproj[e], o_proj) for e in range(NE)])

    def fm(v, n):
        return np.ascontiguousarray(v.reshape(n, P).T, dtype=np.float32)

    d["b_in_t"] = fm(b_in_p, 24)
    d["b_out_t"] = fm(np.asarray(inp["L0_attn_b_out"], np.float32), 8)
    bfc_all = [fm(np.asarray(inp["L0_ffn_bfc_s"], np.float32), 32)]
    bfc_all += [fm(np.asarray(inp["L0_ffn_bfc"], np.float32)[e], 32) for e in range(NE)]
    d["bfc_t"] = np.ascontiguousarray(np.concatenate(bfc_all, axis=1))
    d["bprojs_t"] = fm(np.asarray(inp["L0_ffn_bproj_s"], np.float32), 8)
    d["bprojT_t"] = np.ascontiguousarray(np.asarray(inp["L0_ffn_bproj"], np.float32))
    d["ident_t"] = np.eye(P, dtype=np.float32)
    selT = np.zeros((NH, NC * P), np.float32)
    for c in range(NC):
        for h in (2 * c, 2 * c + 1):
            selT[h, c * P + (h % 2) * HD: c * P + (h % 2) * HD + HD] = 1.0
    d["selT_t"] = selT
    selE = np.zeros((NE, NE * P), np.float32)
    for e in range(NE):
        selE[e, e * P:(e + 1) * P] = 1.0
    d["selE_t"] = selE
    return d


class Plan:
    SELF_SEM = {"vector": "pV", "scalar": "pS", "gpsimd": "pG"}

    def __init__(self, sems):
        self.ops = {e: [] for e in ENGINES}
        self.cnt = dict.fromkeys(sems, 0)
        self.mark = {e: dict.fromkeys(sems, 0) for e in ENGINES}

    def emit(self, eng, fn, waits=(), inc=None, amt=1):
        ss = self.SELF_SEM.get(eng)
        waits = list(waits)
        if ss is not None:
            waits.append((ss, self.cnt[ss]))
            if inc is None:
                inc = ss
            assert inc == ss, (eng, inc)
        w = []
        for (s, v) in waits:
            if v is not None and v > self.mark[eng][s]:
                w.append((s, v))
                self.mark[eng][s] = v
        ms = None
        if inc is not None:
            self.cnt[inc] += amt
            ms = self.cnt[inc]
        self.ops[eng].append((fn, tuple(w), inc, amt))
        return ms


class Banks:
    def __init__(self):
        self.freed = [None] * 8

    def acquire(self, *bs):
        return [self.freed[b] for b in bs if self.freed[b] is not None]

    def release(self, bs, sem, val):
        for b in bs:
            self.freed[b] = (sem, val)


def _build_nc():
    nc = bass.Bass("TRN2", target_bir_lowering=False)

    x_in = nc.dram_tensor("x_in", [L, D], F32, kind="ExternalInput").ap()
    win_s = nc.dram_tensor("win_s", [P, 192 * P], F32, kind="ExternalInput").ap()
    wout_s = nc.dram_tensor("wout_s", [P, 64 * P], F32, kind="ExternalInput").ap()
    wr_s = nc.dram_tensor("wr_s", [P, NC * NE], F32, kind="ExternalInput").ap()
    wfcs_s = nc.dram_tensor("wfcs_s", [P, 256 * P], F32, kind="ExternalInput").ap()
    wprojs_s = nc.dram_tensor("wprojs_s", [P, 256 * P], F32, kind="ExternalInput").ap()
    wfc8_s = nc.dram_tensor("wfc8_s", [NE, P, 256 * P], F32, kind="ExternalInput").ap()
    wproj8_s = nc.dram_tensor("wproj8_s", [NE, P, 256 * P], F32,
                              kind="ExternalInput").ap()
    b_in_t = nc.dram_tensor("b_in_t", [P, 24], F32, kind="ExternalInput").ap()
    b_out_t = nc.dram_tensor("b_out_t", [P, 8], F32, kind="ExternalInput").ap()
    bfc_t = nc.dram_tensor("bfc_t", [P, 288], F32, kind="ExternalInput").ap()
    bprojs_t = nc.dram_tensor("bprojs_t", [P, 8], F32, kind="ExternalInput").ap()
    bprojT_t = nc.dram_tensor("bprojT_t", [NE, D], F32, kind="ExternalInput").ap()
    ident_t = nc.dram_tensor("ident_t", [P, P], F32, kind="ExternalInput").ap()
    selT_t = nc.dram_tensor("selT_t", [NH, NC * P], F32, kind="ExternalInput").ap()
    selE_t = nc.dram_tensor("selE_t", [NE, NE * P], F32, kind="ExternalInput").ap()
    out_d = nc.dram_tensor("out", [L, D], F32, kind="ExternalOutput").ap()

    taps = {}
    if TAPS:
        mk = {
            "xn": ([P, NC * L], BF16), "rtm": ([P, 256], F32),
            "gate": ([P, P], F32), "gate_em": ([NE, L], F32),
            "vs": ([P, NC * L], BF16), "acc": ([P, NC * L], F32),
        }
        for t in TAPS:
            sh, dt = mk[t]
            taps[t] = nc.dram_tensor("tap_" + t, sh, dt, kind="ExternalOutput").ap()

    stack = ExitStack()
    sb = lambda n, s, dt: stack.enter_context(nc.sbuf_tensor(n, s, dt))

    acc = sb("acc", [P, NC * L], F32)
    xn = sb("xn", [P, NC * L], BF16)
    wring = [sb(f"wring{i}", [P, CHUNK_COLS], BF16) for i in range(2)]
    ipbuf = [sb(f"ipring{i}", [P, IP_COLS], BF16) for i in range(2)]
    ident = sb("ident", [P, P], F32)
    consts = sb("consts", [P, 2], F32)
    ones1f = sb("ones1f", [1, P], F32)
    onesPb = sb("onesPb", [P, 1], BF16)
    b_in_sb = sb("b_in_sb", [P, 24], F32)
    b_out_sb = sb("b_out_sb", [P, 8], F32)
    bfc_sb = sb("bfc_sb", [P, 288], F32)
    bprojs_sb = sb("bprojs_sb", [P, 8], F32)
    wr_sb = sb("wr_sb", [P, NC * NE], BF16)
    rtm = sb("rtm", [P, 256], F32)
    rts = sb("rts", [P, 256], F32)
    gtm = sb("gtm", [P, P], F32)
    gem = sb("gem", [NE, L], F32)
    t_s = sb("t_s", [P, TT], F32)
    mxT = [sb(f"mxT{i}", [P, 1], F32) for i in range(2)]
    negT = [sb(f"negT{i}", [P, 1], F32) for i in range(2)]
    sT = [sb(f"sT{i}", [P, 1], F32) for i in range(2)]
    recT = [sb(f"recT{i}", [P, 1], F32) for i in range(2)]
    r1T = [sb(f"r1T{i}", [P, 1], F32) for i in range(2)]
    r2T = [sb(f"r2T{i}", [P, 1], F32) for i in range(2)]
    glg = sb("glg", [P, NE], F32)
    gpr = sb("gpr", [P, NE], F32)
    gm8 = sb("gm8", [P, NE], F32)
    gmsk = sb("gmsk", [P, NE], F32)
    ggu = sb("ggu", [P, NE], F32)
    gg8 = sb("gg8", [P, NE], F32)
    gmxT = sb("gmxT", [P, 1], F32)
    gngT = sb("gngT", [P, 1], F32)
    gseT = sb("gseT", [P, 1], F32)
    grsT = sb("grsT", [P, 1], F32)
    gdnT = sb("gdnT", [P, 1], F32)
    grdT = sb("grdT", [P, 1], F32)
    sdum = sb("sdum", [P, 1], F32)

    early = ExitStack()
    esb = lambda n, s, dt: early.enter_context(nc.sbuf_tensor(n, s, dt))
    xstage = [esb(f"xstage{i}", [P, D], F32) for i in range(2)]
    xb = esb("xb", [P, L], BF16)
    xb2 = esb("xb2", [P, L], BF16)
    sxt = esb("sxt", [1, 512], F32)
    sx2t = esb("sx2t", [1, 512], F32)
    tmpa = esb("tmpa", [1, 512], F32)
    tmpb = esb("tmpb", [1, 512], F32)
    ab = esb("ab", [P, 2 * L], F32)
    t1 = esb("t1", [P, L], F32)

    psum = stack.enter_context(nc.psum_tensor("psarena", [P, 8 * 512], F32))

    def bank(b, n=512, p0=0, p1=P):
        return psum[p0:p1, b * 512: b * 512 + n]

    sems = ("dW0", "dW1", "dI0", "dI1", "dX0", "dX1", "dO0", "dO1", "dB", "dG",
            "pT", "pV", "pS", "pG")
    semh = {s: stack.enter_context(nc.semaphore(s)) for s in sems}
    pl = Plan(sems)
    bk = Banks()

    def dma(eng, fn, sem, waits=()):
        waits = list(waits) + [(sem, pl.cnt[sem])]
        w = []
        for (s, v) in waits:
            if v is not None and v > pl.mark[eng][s]:
                w.append((s, v))
                pl.mark[eng][s] = v
        pl.cnt[sem] += 16
        pl.ops[eng].append((fn, tuple(w), sem, 16))
        return pl.cnt[sem]

    class Ring:
        def __init__(self, slots, sembase, seq):
            self.slots = slots
            self.sems = sembase
            self.seq = seq
            self.issued = 0
            self.load = [None] * len(seq)
            self.rel = [None] * len(seq)

        def issue(self):
            c = self.issued
            waits = []
            if c >= 2:
                rel = self.rel[c - 2]
                assert rel is not None, f"ring chunk {c - 2} not released"
                waits.append(("pT", rel))
            slot = self.slots[c % 2]
            src = self.seq[c]
            self.load[c] = dma(
                "gpsimd", lambda g, s=slot, a=src: g.dma_start(out=s[:, :], in_=a),
                self.sems[c % 2], waits)
            self.issued += 1

        def use(self, c):
            while self.issued <= min(c + 1, len(self.seq) - 1):
                self.issue()
            return (self.sems[c % 2], self.load[c])

        def tile(self, c, i):
            return self.slots[c % 2][:, i * P:(i + 1) * P]

        def done(self, c):
            self.rel[c] = pl.cnt["pT"]

    mseq = []
    ffn_chunk0 = 0
    for t in range(NTT):
        for e in range(NE + 1):
            for i in range(8):
                if e == 0:
                    mseq.append(wfcs_s[:, i * CHUNK_COLS:(i + 1) * CHUNK_COLS])
                else:
                    mseq.append(wfc8_s[e - 1, :, i * CHUNK_COLS:(i + 1) * CHUNK_COLS])
            for i in range(8):
                if e == 0:
                    mseq.append(wprojs_s[:, i * CHUNK_COLS:(i + 1) * CHUNK_COLS])
                else:
                    mseq.append(wproj8_s[e - 1, :, i * CHUNK_COLS:(i + 1) * CHUNK_COLS])
    vin_chunk0 = len(mseq)
    for i in range(2):
        mseq.append(win_s[:, (128 + i * 32) * P:(128 + (i + 1) * 32) * P])
    op_chunk0 = len(mseq)
    for tt in range(NTT):
        for i in range(2):
            mseq.append(wout_s[:, i * CHUNK_COLS:(i + 1) * CHUNK_COLS])
    mring = Ring(wring, ("dW0", "dW1"), mseq)

    ipseq = [win_s[:, g * IP_COLS:(g + 1) * IP_COLS] for g in range(NC)]
    ipring = Ring(ipbuf, ("dI0", "dI1"), ipseq)

    ms_ident = dma("sync", lambda sy: sy.dma_start(
        out=ident[:, :], in_=ident_t[:, :]), "dB")
    pl.emit("gpsimd", lambda g: g.memset(ones1f[:, :], 1.0))
    pl.emit("gpsimd", lambda g: g.memset(onesPb[:, :], 1.0))
    pl.emit("gpsimd", lambda g: g.memset(consts[:, 0:1], 1e-5))
    pl.emit("gpsimd", lambda g: g.memset(consts[:, 1:2], SC))
    ms_setup = pl.emit("gpsimd", lambda g: g.memset(sdum[:, :], 0.0), inc="pG")
    ms_wr = dma("gpsimd", lambda g: g.dma_start(
        out=wr_sb[:, :], in_=wr_s[:, :]), "dG")
    for (dst_, src_) in ((b_in_sb, b_in_t), (b_out_sb, b_out_t), (bfc_sb, bfc_t),
                         (bprojs_sb, bprojs_t)):
        dma("sync", lambda sy, dd=dst_, ss=src_: sy.dma_start(
            out=dd[:, :], in_=ss[:, :]), "dB")
    ms_bias = pl.cnt["dB"]

    tr_banks = (0, 1)
    ms_xT = {}
    for tc in range(NTC):
        slot = xstage[tc % 2]
        waits = [("pT", ms_xT[tc - 2])] if tc >= 2 else []
        ms_ld = dma("sync", lambda sy, s=slot, tc=tc: sy.dma_start(
            out=s[:, :], in_=x_in[tc * P:(tc + 1) * P, :]),
            f"dX{tc % 2}", waits)
        last = None
        for c in range(NC):
            b = tr_banks[c % 2]
            w = [(f"dX{tc % 2}", ms_ld), ("dB", ms_ident),
                 ("pG", ms_setup)] + bk.acquire(b)
            last = pl.emit("tensor", lambda t, s=slot, c=c, b=b: t.transpose(
                bank(b, P), s[:, c * P:(c + 1) * P], ident[:, :]), w, inc="pT")
            msv = pl.emit("vector", lambda v, c=c, tc=tc, b=b: v.tensor_copy(
                acc[:, c * L + tc * P: c * L + (tc + 1) * P], bank(b, P)),
                [("pT", last)], inc="pV")
            bk.release([b], "pV", msv)
        ms_xT[tc] = last
    ms_acc_ready = pl.cnt["pV"]

    stat_w = bk.acquire(*range(8))
    ms_stat_mm = None
    for c in range(NC):
        wcp = [("pV", ms_acc_ready)] + ([("pT", ms_stat_mm)] if c > 0 else [])
        pl.emit("vector", lambda v, c=c: v.tensor_copy(
            xb[:, :], acc[:, c * L:(c + 1) * L]), wcp, inc="pV")
        ms2 = pl.emit("vector", lambda v: v.tensor_mul(
            xb2[:, :], xb[:, :], xb[:, :]), inc="pV")
        for tt in range(NTT):
            w = [("pV", ms2)] + (stat_w if c == 0 else [])
            pl.emit("tensor", lambda t, tt=tt, c=c: t.matmul(
                psum[0:1, tt * 512:(tt + 1) * 512], onesPb[:, :],
                xb[:, tt * 512:(tt + 1) * 512],
                start=(c == 0), stop=(c == NC - 1)), w)
            ms_stat_mm = pl.emit("tensor", lambda t, tt=tt, c=c: t.matmul(
                psum[0:1, (4 + tt) * 512:(5 + tt) * 512], onesPb[:, :],
                xb2[:, tt * 512:(tt + 1) * 512],
                start=(c == 0), stop=(c == NC - 1)), inc="pT")
    for tt in range(NTT):
        pl.emit("vector", lambda v, tt=tt: v.tensor_copy(
            sxt[:, :], psum[0:1, tt * 512:(tt + 1) * 512]),
            [("pT", ms_stat_mm)], inc="pV")
        ms_ev2 = pl.emit("vector", lambda v, tt=tt: v.tensor_copy(
            sx2t[:, :], psum[0:1, (4 + tt) * 512:(5 + tt) * 512]), inc="pV")
        bk.release([tt, 4 + tt], "pV", ms_ev2)
        pl.emit("vector", lambda v: v.tensor_scalar_mul(sxt[:, :], sxt[:, :], 1.0 / D))
        pl.emit("vector", lambda v: v.tensor_scalar_mul(sx2t[:, :], sx2t[:, :],
                                                        1.0 / D))
        pl.emit("vector", lambda v: v.tensor_mul(tmpa[:, :], sxt[:, :], sxt[:, :]))
        ms_var = pl.emit("vector", lambda v: v.tensor_sub(
            sx2t[:, :], sx2t[:, :], tmpa[:, :]), inc="pV")
        ms_std = pl.emit("scalar", lambda s: s.activation(
            tmpa[:, :], sx2t[:, :], AF.Sqrt, bias=consts[0:1, 0:1]),
            [("pV", ms_var), ("pG", ms_setup)], inc="pS")
        pl.emit("vector", lambda v: v.reciprocal(sx2t[:, :], tmpa[:, :]),
                [("pS", ms_std)])
        pl.emit("vector", lambda v: v.tensor_mul(tmpb[:, :], sxt[:, :], sx2t[:, :]))
        ms_bv = pl.emit("vector", lambda v: v.tensor_scalar_mul(
            tmpb[:, :], tmpb[:, :], -1.0), inc="pV")
        for half in range(2):
            b = (tt, 4 + tt)[half]
            srow = sx2t if half == 0 else tmpb
            w = [("pV", ms_bv)] + bk.acquire(b)
            msmm = pl.emit("tensor", lambda t, b=b, s=srow: t.matmul(
                bank(b), ones1f[:, :], s[0:1, :], start=True, stop=True),
                w, inc="pT")
            msv = pl.emit("vector", lambda v, half=half, tt=tt, b=b: v.tensor_copy(
                ab[:, half * L + tt * 512: half * L + tt * 512 + 512], bank(b)),
                [("pT", msmm)], inc="pV")
            bk.release([b], "pV", msv)
    ms_ab = pl.cnt["pV"]

    for c in range(NC):
        pl.emit("vector", lambda v, c=c: v.tensor_mul(
            t1[:, :], acc[:, c * L:(c + 1) * L], ab[:, 0:L]), [("pV", ms_ab)])
        pl.emit("vector", lambda v, c=c: v.tensor_add(
            xn[:, c * L:(c + 1) * L], t1[:, :], ab[:, L:2 * L]), inc="pV")
    ms_xn_all = pl.cnt["pV"]
    if TAPS and "xn" in taps:
        dma("gpsimd", lambda g: g.dma_start(out=taps["xn"][:, :], in_=xn[:, :]),
            "dG", [("pV", ms_xn_all)])

    rtr_banks = (6, 7)
    ms_exp_g = None
    for tc in range(NTC):
        b = rtr_banks[tc % 2]
        w = [("pV", ms_xn_all), ("dG", ms_wr)] + bk.acquire(b)
        last = None
        for k in range(NC):
            last = pl.emit("tensor", lambda t, tc=tc, k=k, b=b: t.matmul(
                bank(b, NE), xn[:, k * L + tc * P: k * L + (tc + 1) * P],
                wr_sb[:, k * NE:(k + 1) * NE],
                start=(k == 0), stop=(k == NC - 1)),
                w if k == 0 else (), inc=("pT" if k == NC - 1 else None))
        ms_lg = pl.emit("vector", lambda v, b=b: v.tensor_copy(
            glg[:, :], bank(b, NE)), [("pT", last)], inc="pV")
        bk.release([b], "pV", ms_lg)
        wmax = [("pS", ms_exp_g)] if ms_exp_g else []
        pl.emit("vector", lambda v: v.reduce_max(gmxT[:, 0:1], glg[:, :],
                                                 axis=AX.X), wmax)
        ms_ng = pl.emit("vector", lambda v: v.tensor_scalar_mul(
            gngT[:, 0:1], gmxT[:, 0:1], -1.0), inc="pV")
        ms_exp_g = pl.emit("scalar", lambda s: s.activation(
            gpr[:, :], glg[:, :], AF.Exp, bias=gngT[:, 0:1],
            accum_out=gseT[:, 0:1]), [("pV", ms_ng)], inc="pS")
        pl.emit("vector", lambda v: v.reciprocal(grsT[:, 0:1], gseT[:, 0:1]),
                [("pS", ms_exp_g)])
        pl.emit("vector", lambda v: v.tensor_scalar_mul(gpr[:, :], gpr[:, :],
                                                        grsT[:, 0:1]))
        pl.emit("vector", lambda v: v.max(out=gm8[:, :], in_=gpr[:, :]))
        pl.emit("vector", lambda v: v.tensor_single_scalar(
            out=gmsk[:, :], in_=gpr[:, :], scalar=gm8[:, 1:2], op=ALU.is_ge))
        pl.emit("vector", lambda v: v.tensor_mul(ggu[:, :], gpr[:, :], gmsk[:, :]))
        pl.emit("vector", lambda v: v.tensor_add(gdnT[:, 0:1], gm8[:, 0:1],
                                                 gm8[:, 1:2]))
        pl.emit("vector", lambda v: v.reciprocal(grdT[:, 0:1], gdnT[:, 0:1]))
        pl.emit("vector", lambda v: v.tensor_scalar_mul(
            gg8[:, :], ggu[:, :], grdT[:, 0:1]))
        for e in range(NE):
            ms_g = pl.emit("vector", lambda v, tc=tc, e=e: v.tensor_copy(
                gtm[:, e * NTC + tc: e * NTC + tc + 1], gg8[:, e:e + 1]),
                inc="pV")
    ms_gate_tm = ms_g
    if TAPS and "gate" in taps:
        dma("sync", lambda sy: sy.dma_start(out=taps["gate"][:, :], in_=gtm[:, :]),
            "dB", [("pV", ms_gate_tm)])
    bT = 6
    w = [("pV", ms_gate_tm)] + bk.acquire(bT)
    ms_gT = pl.emit("tensor", lambda t: t.transpose(
        bank(bT, P), gtm[:, :], ident[:, :]), w, inc="pT")
    ms_gTc = pl.emit("vector", lambda v: v.tensor_copy(rts[:, 0:P], bank(bT, P)),
                     [("pT", ms_gT)], inc="pV")
    bk.release([bT], "pV", ms_gTc)
    for e in range(NE):
        dma("sync", lambda sy, e=e: sy.dma_start(
            out=gem[e:e + 1, :], in_=rts[e * NTC:(e + 1) * NTC, 0:P]),
            "dB", [("pV", ms_gTc)])
    ms_gem_all = pl.cnt["dB"]
    if TAPS and "gate_em" in taps:
        dma("sync", lambda sy: sy.dma_start(out=taps["gate_em"][:, :], in_=gem[:, :]),
            "dB", [("dB", ms_gem_all)])

    early.close()
    stageB = ExitStack()
    bsb = lambda n, s, dt: stageB.enter_context(nc.sbuf_tensor(n, s, dt))
    ebuf = bsb("ebuf", [P, L], F32)
    qg = bsb("qg", [P, 2 * L], BF16)
    kg = bsb("kg", [P, 2 * L], BF16)
    stageC = ExitStack()
    csb = lambda n, s, dt: stageC.enter_context(nc.sbuf_tensor(n, s, dt))
    bprojT_sb = csb("bprojT_sb", [NE, D], F32)
    selE = csb("selE", [NE, NE * P], F32)
    g_sb = csb("g_sb", [P, TT], F32)
    t_v = csb("t_v", [P, TT], F32)
    hbuf = csb("hbuf", [P, NHC * TT], BF16)
    ms_bpT = dma("sync", lambda sy: sy.dma_start(
        out=bprojT_sb[:, :], in_=bprojT_t[:, :]), "dB")
    ms_selE = dma("sync", lambda sy: sy.dma_start(
        out=selE[:, :], in_=selE_t[:, :]), "dB")

    scores_ap = psum[:, 4 * 512: 8 * 512]
    prev_a1 = None
    prev_s = [None, None]
    ms_ip_done = [None] * NC
    ms_qk_last = [None] * NC
    unit_ctr = [0]

    def emit_burst(g):
        wready = ipring.use(g)
        waits_war = []
        if g >= 2 and ms_qk_last[g - 2] is not None:
            waits_war.append(("pT", ms_qk_last[g - 2]))
        col0 = (g % 2) * L
        for j in range(2):
            dstbuf = qg if j == 0 else kg
            bias_col = g if j == 0 else 8 + g
            w0 = [wready, ("pV", ms_xn_all), ("dB", ms_bias)] \
                + bk.acquire(4, 5, 6, 7) + waits_war
            lasts = [None] * NTT
            for k in range(NC):
                for tt in range(NTT):
                    ti = j * 8 + k
                    lasts[tt] = pl.emit(
                        "tensor", lambda t, g=g, ti=ti, k=k, tt=tt: t.matmul(
                            bank(4 + tt), ipring.tile(g, ti),
                            xn[:, k * L + tt * 512: k * L + tt * 512 + 512],
                            start=(k == 0), stop=(k == NC - 1)),
                        w0 if (k == 0 and tt == 0) else (),
                        inc=("pT" if k == NC - 1 else None))
            for tt in range(NTT):
                if j == 0:
                    msev = pl.emit("vector",
                                   lambda v, tt=tt, col0=col0, dstbuf=dstbuf,
                                   bc=bias_col: v.tensor_scalar_add(
                                       dstbuf[:, col0 + tt * 512:
                                              col0 + tt * 512 + 512],
                                       bank(4 + tt), b_in_sb[:, bc:bc + 1]),
                                   [("pT", lasts[tt])], inc="pV")
                else:
                    msev = pl.emit("vector",
                                   lambda v, tt=tt, col0=col0, dstbuf=dstbuf,
                                   bc=bias_col: v.tensor_scalar(
                                       out=dstbuf[:, col0 + tt * 512:
                                                  col0 + tt * 512 + 512],
                                       in0=bank(4 + tt),
                                       scalar1=b_in_sb[:, bc:bc + 1],
                                       scalar2=SC, op0=ALU.add, op1=ALU.mult),
                                   [("pT", lasts[tt])], inc="pV")
                bk.release([4 + tt], "pV", msev)
        ms_ip_done[g] = pl.cnt["pV"]
        ipring.done(g)

    def emit_qk_unit(g, hh, lc):
        nonlocal prev_a1
        u = unit_ctr[0] % 2
        unit_ctr[0] += 1
        pb = hh * HD
        col0 = (g % 2) * L
        w = [("pV", ms_ip_done[g])] + bk.acquire(4, 5, 6, 7)
        last = None
        for mt in range(4):
            last = pl.emit("tensor", lambda t, pb=pb, col0=col0, lc=lc, mt=mt:
                           t.matmul(bank(4 + mt),
                                    qg[pb:pb + HD,
                                       col0 + lc * P: col0 + (lc + 1) * P],
                                    kg[pb:pb + HD,
                                       col0 + mt * 512: col0 + mt * 512 + 512],
                                    start=True, stop=True),
                           w if mt == 0 else (),
                           inc=("pT" if mt == 3 else None))
        ms_qk_last[g] = last
        wv = [("pT", last)]
        if prev_s[u] is not None:
            wv.append(("pS", prev_s[u]))
        pl.emit("vector", lambda v, u=u: v.reduce_max(
            mxT[u][:, 0:1], scores_ap, axis=AX.X), wv)
        ms_ng = pl.emit("vector", lambda v, u=u: v.tensor_scalar_mul(
            negT[u][:, 0:1], mxT[u][:, 0:1], -1.0), inc="pV")
        we = [("pV", ms_ng)]
        if prev_a1 is not None:
            we.append(("pV", prev_a1))
        ms_exp = pl.emit("scalar", lambda s, u=u: s.activation(
            ebuf[:, :], scores_ap, AF.Exp, bias=negT[u][:, 0:1],
            accum_out=sT[u][:, 0:1]), we, inc="pS")
        bk.release([4, 5, 6, 7], "pS", ms_exp)
        ms_rc = pl.emit("vector", lambda v, u=u: v.reciprocal(
            recT[u][:, 0:1], sT[u][:, 0:1]), [("pS", ms_exp)], inc="pV")
        prev_a1 = pl.emit("vector", lambda v, u=u: v.tensor_scalar(
            out=ebuf[:, 0:1024], in0=ebuf[:, 0:1024], scalar1=recT[u][:, 0:1],
            scalar2=0.0, op0=ALU.mult, op1=ALU.add,
            accum_out=r1T[u][:, 0:1]), inc="pV")
        ms_a2 = pl.emit("scalar", lambda s, u=u: s.activation(
            ebuf[:, 1024:2048], ebuf[:, 1024:2048], AF.Copy,
            scale=recT[u][:, 0:1], accum_out=r2T[u][:, 0:1]),
            [("pV", ms_rc)], inc="pS")
        prev_s[u] = ms_a2
        h = 2 * g + hh
        pl.emit("vector", lambda v, u=u, h=h, lc=lc: v.tensor_add(
            rtm[:, h * NTC + lc: h * NTC + lc + 1], r1T[u][:, 0:1],
            r2T[u][:, 0:1]), [("pS", ms_a2)], inc="pV")

    burst_at = {0: [0, 1], 4: [2], 8: [3], 12: [4], 16: [5], 20: [6], 24: [7]}
    qk_queue = []

    fc_pairs = ((0, 1), (2, 3))
    ms_h_prev_read = None
    for t in range(NTT):
        for e in range(NE + 1):
            unit = t * (NE + 1) + e
            for g in burst_at.get(unit, []):
                emit_burst(g)
                qk_queue.extend((g, hh, lc) for hh in range(2)
                                for lc in range(NTC))
            c0 = ffn_chunk0 + unit * 16
            msgc = None
            if e > 0:
                bG = 4
                w = [("dB", ms_gem_all), ("dB", ms_selE)] + bk.acquire(bG)
                msg = pl.emit("tensor", lambda tn, e=e, t=t: tn.matmul(
                    bank(bG), selE[:, (e - 1) * P: e * P],
                    gem[:, t * 512:(t + 1) * 512],
                    start=True, stop=True), w, inc="pT")
                msgc = pl.emit("vector", lambda v: v.tensor_copy(
                    g_sb[:, :], bank(bG)), [("pT", msg)], inc="pV")
                bk.release([bG], "pV", msgc)
            for g2 in range(NHC // 2):
                cc = c0 + g2 // 2
                wready = mring.use(cc)
                base_ti = (g2 % 2) * 16
                pair = fc_pairs[g2 % 2]
                for j, hm in enumerate((2 * g2, 2 * g2 + 1)):
                    b = pair[j]
                    w = [wready, ("pV", ms_xn_all)] + bk.acquire(b)
                    if ms_h_prev_read is not None:
                        w.append(("pT", ms_h_prev_read))
                    last = None
                    for k in range(NC):
                        ti = base_ti + k * 2 + j
                        last = pl.emit("tensor",
                                       lambda tn, cc=cc, ti=ti, k=k, t=t, b=b:
                                       tn.matmul(bank(b), mring.tile(cc, ti),
                                                 xn[:, k * L + t * 512:
                                                    k * L + t * 512 + 512],
                                                 start=(k == 0),
                                                 stop=(k == NC - 1)),
                                       w if k == 0 else (),
                                       inc=("pT" if k == NC - 1 else None))
                    msgl = pl.emit("scalar", lambda s, b=b, hm=hm, e=e: s.activation(
                        hbuf[:, hm * 512:(hm + 1) * 512], bank(b),
                        AF.Gelu_apprx_tanh,
                        bias=bfc_sb[:, e * 32 + hm: e * 32 + hm + 1]),
                        [("pT", last)], inc="pS")
                    bk.release([b], "pS", msgl)
                if g2 % 2 == 1:
                    mring.done(cc)
                    if qk_queue:
                        emit_qk_unit(*qk_queue.pop(0))
            ms_gelu_all = pl.cnt["pS"]
            h_read_last = None
            for quad in range(2):
                qb = [4, 5, 6, 7]
                seeded = (e == 0)
                lasts = [None] * 4
                for k in range(NHC):
                    cck = c0 + 8 + (quad * NHC * 4 + k * 4) // CHUNK_TILES
                    wready = mring.use(cck)
                    for i in range(4):
                        m = quad * 4 + i
                        b = qb[i]
                        ti = (k * 4 + i) % CHUNK_TILES
                        if k == 0:
                            w = [wready, ("pS", ms_gelu_all)] + bk.acquire(b)
                            if seeded:
                                pl.emit("tensor", lambda tn, m=m, t=t, b=b:
                                        tn.matmul(bank(b),
                                                  bprojT_sb[:, m * P:(m + 1) * P],
                                                  gem[:, t * 512:(t + 1) * 512],
                                                  start=True, stop=False),
                                        w + [("dB", max(ms_gem_all, ms_bpT))])
                                w = []
                        else:
                            w = [wready]
                        lasts[i] = pl.emit(
                            "tensor",
                            lambda tn, cck=cck, ti=ti, k=k, b=b, seeded=seeded:
                            tn.matmul(bank(b), mring.tile(cck, ti),
                                      hbuf[:, k * 512:(k + 1) * 512],
                                      start=(k == 0 and not seeded),
                                      stop=(k == NHC - 1)),
                            w, inc=("pT" if (k == NHC - 1 or
                                             (k % 8 == 7 and i == 3)) else None))
                    if k % 8 == 7:
                        mring.done(c0 + 8 + quad * 4 + k // 8)
                h_read_last = lasts[3]
                for i in range(4):
                    m = quad * 4 + i
                    b = qb[i]
                    if e == 0:
                        msad = pl.emit(
                            "vector", lambda v, b=b, m=m, t=t: v.scalar_tensor_tensor(
                                acc[:, m * L + t * 512: m * L + t * 512 + 512],
                                bank(b), bprojs_sb[:, m:m + 1],
                                acc[:, m * L + t * 512: m * L + t * 512 + 512],
                                ALU.add, ALU.add),
                            [("pT", lasts[i])], inc="pV")
                    else:
                        pl.emit("vector", lambda v, b=b: v.tensor_tensor(
                            t_v[:, :], g_sb[:, :], bank(b), ALU.mult),
                            [("pT", lasts[i]), ("pV", msgc)], inc="pV")
                        msad = pl.emit("vector", lambda v, m=m, t=t: v.tensor_add(
                            acc[:, m * L + t * 512: m * L + t * 512 + 512],
                            acc[:, m * L + t * 512: m * L + t * 512 + 512],
                            t_v[:, :]), inc="pV")
                    bk.release([b], "pV", msad)
            ms_h_prev_read = h_read_last

    while qk_queue:
        emit_qk_unit(*qk_queue.pop(0))
    ms_r_all = pl.cnt["pV"]

    stageC.close()
    stageB.close()
    stageD = ExitStack()
    dsb = lambda n, s, dt: stageD.enter_context(nc.sbuf_tensor(n, s, dt))
    vbuf = dsb("vbuf", [P, NC * L], BF16)
    selT = dsb("selT", [NH, NC * P], F32)
    rh = dsb("rh", [NH, L], F32)
    ostage = [dsb(f"ostage{i}", [P, D], F32) for i in range(2)]
    ms_tail_bar = (("pS", pl.cnt["pS"]), ("pV", pl.cnt["pV"]),
                   ("pT", pl.cnt["pT"]))
    ms_selT = dma("sync", lambda sy: sy.dma_start(
        out=selT[:, :], in_=selT_t[:, :]), "dB",
        [("pT", pl.cnt["pT"]), ("pS", pl.cnt["pS"]), ("pV", pl.cnt["pV"])])

    ms_v_all = None
    for j in range(NC):
        c = vin_chunk0 + j // 4
        wready = mring.use(c)
        bs = 0 if j % 2 == 0 else 4
        w0 = [wready, ("pV", ms_xn_all), ("dB", ms_bias)] \
            + bk.acquire(bs, bs + 1, bs + 2, bs + 3) + list(ms_tail_bar)
        lasts = [None] * NTT
        for k in range(NC):
            for tt in range(NTT):
                ti = (j % 4) * 8 + k
                lasts[tt] = pl.emit("tensor", lambda t, c=c, ti=ti, k=k, tt=tt,
                                    bs=bs: t.matmul(
                                        bank(bs + tt), mring.tile(c, ti),
                                        xn[:, k * L + tt * 512:
                                           k * L + tt * 512 + 512],
                                        start=(k == 0), stop=(k == NC - 1)),
                                    w0 if (k == 0 and tt == 0) else (),
                                    inc=("pT" if k == NC - 1 else None))
        for tt in range(NTT):
            ms_v_all = pl.emit("vector", lambda v, j=j, tt=tt, bs=bs:
                               v.tensor_scalar_add(
                                   vbuf[:, j * L + tt * 512:
                                        j * L + tt * 512 + 512],
                                   bank(bs + tt), b_in_sb[:, 16 + j:17 + j]),
                               [("pT", lasts[tt])] + list(ms_tail_bar),
                               inc="pV")
            bk.release([bs + tt], "pV", ms_v_all)
        if j == 3:
            mring.done(vin_chunk0)
    mring.done(vin_chunk0 + 1)

    ms_rts = None
    for half in range(2):
        b = half % 2
        w = [("pV", ms_r_all)] + bk.acquire(b)
        msT = pl.emit("tensor", lambda t, half=half, b=b: t.transpose(
            bank(b, P), rtm[:, half * P:(half + 1) * P], ident[:, :]), w, inc="pT")
        ms_rts = pl.emit("vector", lambda v, half=half, b=b: v.tensor_copy(
            rts[:, half * P:(half + 1) * P], bank(b, P)),
            [("pT", msT), ("dB", ms_gem_all)], inc="pV")
        bk.release([b], "pV", ms_rts)
    for h in range(NH):
        half, row0 = (0, h * NTC) if h < NH // 2 else (1, (h - NH // 2) * NTC)
        dma("sync", lambda sy, h=h, half=half, row0=row0: sy.dma_start(
            out=rh[h:h + 1, :],
            in_=rts[row0:row0 + NTC, half * P:(half + 1) * P]),
            "dB", [("pV", ms_rts)])
    ms_rh = pl.cnt["dB"]
    if TAPS and "rtm" in taps:
        dma("sync", lambda sy: sy.dma_start(out=taps["rtm"][:, :], in_=rtm[:, :]),
            "dB", [("pV", ms_r_all)])

    vs_banks = (2, 3)
    ms_vs_all = None
    for c in range(NC):
        for tt in range(NTT):
            b = vs_banks[(c * NTT + tt) % 2]
            w = [("dB", ms_rh), ("dB", ms_selT), ("pV", ms_v_all)] + bk.acquire(b)
            msb = pl.emit("tensor", lambda t, c=c, tt=tt, b=b: t.matmul(
                bank(b), selT[:, c * P:(c + 1) * P],
                rh[:, tt * 512:(tt + 1) * 512], start=True, stop=True),
                w, inc="pT")
            ms_vs_all = pl.emit("vector", lambda v, c=c, tt=tt, b=b: v.tensor_tensor(
                vbuf[:, c * L + tt * 512: c * L + tt * 512 + 512],
                vbuf[:, c * L + tt * 512: c * L + tt * 512 + 512],
                bank(b), ALU.mult), [("pT", msb)], inc="pV")
            bk.release([b], "pV", ms_vs_all)
    if TAPS and "vs" in taps:
        dma("gpsimd", lambda g: g.dma_start(out=taps["vs"][:, :], in_=vbuf[:, :]),
            "dG", [("pV", ms_vs_all)])

    ms_store = {}
    for tt in range(NTT):
        for m in range(NC):
            c = op_chunk0 + tt * 2 + m // 4
            wready = mring.use(c)
            b = m
            w0 = [wready, ("pV", ms_vs_all)] + bk.acquire(b)
            last = None
            for k in range(NC):
                ti = (m % 4) * 8 + k
                last = pl.emit("tensor", lambda t, c=c, ti=ti, k=k, tt=tt, b=b:
                               t.matmul(bank(b), mring.tile(c, ti),
                                        vbuf[:, k * L + tt * 512:
                                             k * L + tt * 512 + 512],
                                        start=(k == 0), stop=(k == NC - 1)),
                               w0 if k == 0 else (),
                               inc=("pT" if k == NC - 1 else None))
            msad = pl.emit("vector", lambda v, m=m, tt=tt, b=b:
                           v.scalar_tensor_tensor(
                               acc[:, m * L + tt * 512: m * L + tt * 512 + 512],
                               bank(b), b_out_sb[:, m:m + 1],
                               acc[:, m * L + tt * 512: m * L + tt * 512 + 512],
                               ALU.add, ALU.add),
                           [("pT", last)], inc="pV")
            bk.release([b], "pV", msad)
            if m == 3 or m == 7:
                mring.done(c)
        for tc in range(tt * 4, tt * 4 + 4):
            slot = ostage[tc % 2]
            msc = None
            for cch in range(NC):
                b = cch % 2
                w = [("pV", msad)] + bk.acquire(b)
                msT = pl.emit("tensor", lambda t, cch=cch, tc=tc, b=b: t.transpose(
                    bank(b, P), acc[:, cch * L + tc * P: cch * L + (tc + 1) * P],
                    ident[:, :]), w, inc="pT")
                wv = [("pT", msT)]
                if tc >= 2:
                    wv.append((f"dO{tc % 2}", ms_store[tc - 2]))
                msc = pl.emit("vector", lambda v, s=slot, cch=cch, b=b:
                              v.tensor_copy(s[:, cch * P:(cch + 1) * P],
                                            bank(b, P)), wv, inc="pV")
                bk.release([b], "pV", msc)
            ms_store[tc] = dma("sync", lambda sy, s=slot, tc=tc: sy.dma_start(
                out=out_d[tc * P:(tc + 1) * P, :], in_=s[:, :]),
                f"dO{tc % 2}", [("pV", msc)])
    assert mring.issued == len(mseq), (mring.issued, len(mseq))
    assert ipring.issued == len(ipseq)

    ms_acc_final = pl.cnt["pV"]
    if TAPS and "acc" in taps:
        dma("sync", lambda sy: sy.dma_start(out=taps["acc"][:, :], in_=acc[:, :]),
            "dB", [("pV", ms_acc_final)])
    stageD.close()

    pl.emit("sync", lambda sy: sy.nop(),
            [(s, pl.cnt[s]) for s in
             ("dW0", "dW1", "dI0", "dI1", "dX0", "dX1", "dO0", "dO1", "dB", "dG")])

    with nc.Block() as block:
        for engname in ENGINES:
            def _emit(eng, _ops=pl.ops[engname]):
                for fn, waits, inc, amt in _ops:
                    for (s, v) in waits:
                        eng.wait_ge(semh[s], v)
                    instr = fn(eng)
                    if inc is not None:
                        instr.then_inc(semh[inc], amt)
            getattr(block, engname)(_emit)

    stack.close()
    nc.finalize()
    return nc


def kernel(**inputs):
    if "nc" not in _CACHE:
        _CACHE["nc"] = _build_nc()
    nc = _CACHE["nc"]
    d = _host_layout(inputs)
    x = np.ascontiguousarray(np.asarray(inputs["L0_attn_x"], np.float32))
    in_maps = []
    for b in range(8):
        m = dict(d)
        m["x_in"] = np.ascontiguousarray(x[b])
        in_maps.append(m)
    res = run_bass_kernel_spmd(nc, in_maps, core_ids=list(range(8)))
    out = np.stack([res.results[b]["out"] for b in range(8)], axis=0)
    if TAPS:
        kernel.tap_results = res.results
    return out
